# version 01 (final)
# baseline (speedup 1.0000x reference)
import numpy as np
import ml_dtypes
import concourse.bass as bass
import concourse.mybir as mybir
from concourse.bass_utils import run_bass_kernel_spmd

F32 = mybir.dt.float32
BF16 = mybir.dt.bfloat16
AF = mybir.ActivationFunctionType
ALU = mybir.AluOpType
AX = mybir.AxisListType
BF = ml_dtypes.bfloat16

B, L, D = 2, 8192, 1024
DI, DQ, NDB, NST, DTR = 2048, 512, 4, 16, 64
LC, NCH = 1024, 8
DBG = False
TRUNC = 0
EPS = 1e-5
NH, NKV, HD = 16, 4, 64
NE, FF = 8, 2048

ENG_SEM = {"sync": "sq", "vector": "sv", "scalar": "sa", "tensor": "st",
           "gpsimd": "sg"}
ENGS = ("sync", "vector", "scalar", "tensor", "gpsimd")


class Ctx:

    def __init__(self, sems, active, eng):
        self.sems = sems
        self.active = active
        self.eng = eng
        self.counts = {name: 0 for name in sems}

    def c(self, sem):
        return self.counts[sem]

    def do(self, engname, fn, waits=(), inc=1, sem=None):
        semname = sem if sem is not None else ENG_SEM[engname]
        if self.active == engname:
            e = self.eng
            for sname, val in waits:
                if val > 0:
                    e.wait_ge(self.sems[sname], val)
            inst = fn(e)
            if inc:
                inst.then_inc(self.sems[semname], inc)
        self.counts[semname] += inc
        return self.counts[semname]

    def dma(self, engname, stream, out, in_, waits=()):
        prior = self.counts[stream]
        w = list(waits)
        if prior > 0:
            w.append((stream, prior))
        return self.do(engname, lambda e: e.dma_start(out=out, in_=in_),
                       waits=w, inc=16, sem=stream)

    def wait(self, engname, waits):
        if self.active == engname:
            for sname, val in waits:
                if val > 0:
                    self.eng.wait_ge(self.sems[sname], val)


def register(block, sems, program):
    @block.sync
    def _(e):
        program(Ctx(sems, "sync", e))

    @block.vector
    def _(e):
        program(Ctx(sems, "vector", e))

    @block.scalar
    def _(e):
        program(Ctx(sems, "scalar", e))

    @block.tensor
    def _(e):
        program(Ctx(sems, "tensor", e))

    @block.gpsimd
    def _(e):
        program(Ctx(sems, "gpsimd", e))


def build_phase_a():
    nc = bass.Bass("TRN2", target_bir_lowering=False, debug=False,
                   detect_race_conditions=False)

    xb = nc.dram_tensor("xb", [L, D], F32, kind="ExternalInput").ap()
    w_in_T = nc.dram_tensor("w_in_T", [D, 1024], F32, kind="ExternalInput").ap()
    conv_diag = nc.dram_tensor("conv_diag", [128, NDB * 4 * 128], F32, kind="ExternalInput").ap()
    xp_T = nc.dram_tensor("xp_T", [DQ, 96], F32, kind="ExternalInput").ap()
    dt_T = nc.dram_tensor("dt_T", [DTR, DQ], F32, kind="ExternalInput").ap()
    dt_b = nc.dram_tensor("dt_b", [128, NDB], F32, kind="ExternalInput").ap()
    a_log = nc.dram_tensor("a_log", [128, NDB * NST], F32, kind="ExternalInput").ap()
    dp_c = nc.dram_tensor("dp_c", [128, NDB], F32, kind="ExternalInput").ap()
    convb_c = nc.dram_tensor("convb_c", [128, NDB], F32, kind="ExternalInput").ap()
    normw_c = nc.dram_tensor("normw_c", [128, 8], F32, kind="ExternalInput").ap()
    ident = nc.dram_tensor("ident", [128, 128], BF16, kind="ExternalInput").ap()

    ys = nc.dram_tensor("ys", [8, DQ, LC], BF16, kind="ExternalOutput").ap()
    if DBG:
        dbg_outs = {
            "xnf_d": (nc.dram_tensor("xnf_d", [128, 8 * 1024], BF16, kind="ExternalOutput").ap(), "xnf"),
            "zb_d": (nc.dram_tensor("zb_d", [128, NDB * LC], F32, kind="ExternalOutput").ap(), "zb"),
            "xc_d": (nc.dram_tensor("xc_d", [128, NDB * LC], F32, kind="ExternalOutput").ap(), "xc"),
            "dblr_d": (nc.dram_tensor("dblr_d", [128, LC], F32, kind="ExternalOutput").ap(), "dblr"),
            "bcrf_d": (nc.dram_tensor("bcrf_d", [32, LC], F32, kind="ExternalOutput").ap(), "bcrf"),
            "dlt_d": (nc.dram_tensor("dlt_d", [128, NDB * LC], F32, kind="ExternalOutput").ap(), "dlt"),
            "xhat_d": (nc.dram_tensor("xhat_d", [128, 2 * 1024], BF16, kind="ExternalOutput").ap(), "xhat"),
            "stat_d": (nc.dram_tensor("stat_d", [128, 24], F32, kind="ExternalOutput").ap(), "stat"),
            "dub_d": (nc.dram_tensor("dub_d", [128, NDB * LC], BF16, kind="ExternalOutput").ap(), "dub"),
            "st8_d": (nc.dram_tensor("st8_d", [128, NDB * NST], F32, kind="ExternalOutput").ap(), "st8"),
        }

    arin = [nc.dram_tensor(f"arin{i}", [96, LC], F32) for i in range(NCH)]
    arout = [nc.dram_tensor(f"arout{i}", [96, LC], F32) for i in range(NCH)]

    from contextlib import ExitStack
    with ExitStack() as _stk:
        ent = _stk.enter_context
        wst = ent(nc.sbuf_tensor("wst", [128, 2 * 1024], F32))
        w_in = ent(nc.sbuf_tensor("w_in", [128, 8 * 1024], BF16))
        cdiag = ent(nc.sbuf_tensor("cdiag", [128, NDB * 4 * 128], BF16))
        xp_w = ent(nc.sbuf_tensor("xp_w", [128, NDB * 96], BF16))
        dt_w = ent(nc.sbuf_tensor("dt_w", [DTR, NDB * 128], BF16))
        dtb = ent(nc.sbuf_tensor("dtb", [128, NDB], F32))
        acol = ent(nc.sbuf_tensor("acol", [128, NDB * NST], F32))
        dpc = ent(nc.sbuf_tensor("dpc", [128, NDB], F32))
        cvb = ent(nc.sbuf_tensor("cvb", [128, NDB], F32))
        nwc = ent(nc.sbuf_tensor("nwc", [128, 8], F32))
        idt = ent(nc.sbuf_tensor("idt", [128, 128], BF16))
        ones32 = ent(nc.sbuf_tensor("ones32", [65, 128], BF16))
        bcr1 = ent(nc.sbuf_tensor("bcr1", [65, 11 * LC], BF16))
        xst = ent(nc.sbuf_tensor("xst", [128, 2 * 1024], F32))
        xhat = ent(nc.sbuf_tensor("xhat", [128, 2 * 1024], BF16))
        stat = ent(nc.sbuf_tensor("stat", [128, 24], F32))
        xnf = ent(nc.sbuf_tensor("xnf", [128, 8 * 1024], BF16))
        xi = ent(nc.sbuf_tensor("xi", [128, NDB * (LC + 3)], BF16))
        zb = ent(nc.sbuf_tensor("zb", [128, NDB * LC], F32))
        xc = ent(nc.sbuf_tensor("xc", [128, NDB * LC], F32))
        xcb = ent(nc.sbuf_tensor("xcb", [128, NDB * LC], BF16))
        dlt = ent(nc.sbuf_tensor("dlt", [128, NDB * LC], F32))
        dub = ent(nc.sbuf_tensor("dub", [128, NDB * LC], BF16))
        dbl_s = ent(nc.sbuf_tensor("dbl_s", [128, LC], F32))
        dblr = ent(nc.sbuf_tensor("dblr", [128, LC], F32))
        dtin = ent(nc.sbuf_tensor("dtin", [DTR, LC], BF16))
        bcrf = ent(nc.sbuf_tensor("bcrf", [32, LC], F32))
        bcrb = ent(nc.sbuf_tensor("bcrb", [32, LC], BF16))
        bbc = ent(nc.sbuf_tensor("bbc", [128, 2 * 512], BF16))
        cbc = ent(nc.sbuf_tensor("cbc", [128, 2 * 512], BF16))
        dA = ent(nc.sbuf_tensor("dA", [128, 2 * 512], F32))
        dBu = ent(nc.sbuf_tensor("dBu", [128, 2 * 512], BF16))
        hs = ent(nc.sbuf_tensor("hs", [128, 2 * 512], F32))
        prodb = ent(nc.sbuf_tensor("prodb", [128, 2 * 512], BF16))
        st8 = ent(nc.sbuf_tensor("st8", [128, NDB * NST], F32))
        y1 = ent(nc.sbuf_tensor("y1", [128, 512], F32))
        sp1 = ent(nc.sbuf_tensor("sp1", [128, 512], F32))
        sp2 = ent(nc.sbuf_tensor("sp2", [128, 512], F32))
        zact = ent(nc.sbuf_tensor("zact", [128, 2 * 512], BF16))
        yg = ent(nc.sbuf_tensor("yg", [128, NDB * LC], BF16))
        py = ent(nc.psum_tensor("py", [128, 4 * 512], F32))
        pmm = ent(nc.psum_tensor("pmm", [128, 512], F32))
        ptr = ent(nc.psum_tensor("ptr", [128, 2 * 1024], BF16))
        pbc = ent(nc.psum_tensor("pbc", [128, 512], F32))
        sems = {}
        for _sn in ("sv", "sa", "st", "sc", "wq", "x0", "x1", "dr", "ab", "b1", "yo"):
            sems[_sn] = ent(nc.semaphore(_sn))
        block = ent(nc.Block())

        def program(ctx: Ctx):
            for kk in range(8):
                h = (kk % 2) * 1024
                ctx.dma("sync", "wq", wst[:, h:h + 1024], w_in_T[kk * 128:(kk + 1) * 128, :],
                        waits=[("sv", ctx.c("sv") if kk >= 2 else 0)])
                ctx.do("vector", lambda e, kk=kk, h=h: e.tensor_copy(
                    out=w_in[:, kk * 1024:(kk + 1) * 1024], in_=wst[:, h:h + 1024]),
                    waits=[("wq", ctx.c("wq"))])
            ctx.dma("sync", "wq", wst[:, 0:2048], conv_diag, waits=[("sv", ctx.c("sv"))])
            ctx.do("vector", lambda e: e.tensor_copy(out=cdiag[:, :], in_=wst[:, 0:2048]),
                   waits=[("wq", ctx.c("wq"))])
            for db in range(NDB):
                h = (db % 2) * 1024
                ctx.dma("sync", "wq", wst[:, h:h + 96], xp_T[db * 128:(db + 1) * 128, :],
                        waits=[("sv", ctx.c("sv"))])
                ctx.do("vector", lambda e, db=db, h=h: e.tensor_copy(
                    out=xp_w[:, db * 96:(db + 1) * 96], in_=wst[:, h:h + 96]),
                    waits=[("wq", ctx.c("wq"))])
            ctx.dma("sync", "wq", wst[0:64, 0:512], dt_T, waits=[("sv", ctx.c("sv"))])
            ctx.do("vector", lambda e: e.tensor_copy(out=dt_w[:, :], in_=wst[0:64, 0:512]),
                   waits=[("wq", ctx.c("wq"))])
            for dst, srct in [(dtb[:, :], dt_b), (acol[:, :], a_log), (dpc[:, :], dp_c),
                             (cvb[:, :], convb_c), (nwc[:, :], normw_c), (idt[:, :], ident)]:
                ctx.dma("sync", "wq", dst, srct)
            ctx.do("scalar", lambda e: e.activation(out=acol[:, :], in_=acol[:, :], func=AF.Exp),
                   waits=[("wq", ctx.c("wq"))])
            ctx.do("vector", lambda e: e.tensor_scalar(
                out=acol[:, :], in0=acol[:, :], scalar1=-1.0, scalar2=None, op0=ALU.mult),
                waits=[("sa", ctx.c("sa"))])
            ctx.do("vector", lambda e: e.memset(st8[:, :], 0.0))
            ctx.do("vector", lambda e: e.memset(ones32[:, :], 1.0))

            def zero_halo(e):
                for db in range(NDB):
                    inst = e.memset(xi[:, db * (LC + 3):db * (LC + 3) + 3], 0.0)
                return inst
            ctx.do("vector", zero_halo)
            wq_done = ctx.c("wq")

            pv = dict(inproj_mm=0, inproj_ev=0, conv_mm=0, scan_all_v=0, scan_all_a=0, yg_v=0,
                      ab_mark=0, yo_mark=0,
                      xq_mm=0, dtin_v=0, bcr_mm=0, xcb_st=0)
            xhat_vs = []
            trev_sas = []
            trmm_sts = []
            state_marks = {}
            scan_vs = []
            idmm_sts = []
            build_vs = []
            prodn_vs = []
            gate_vs = []

            for ch in range(NCH):
                l0 = ch * LC
                for tt in range(LC // 128):
                    gtile = ch * 8 + tt
                    hh = (tt % 2) * 1024
                    xsl = xst[:, hh:hh + 1024]
                    xhl = xhat[:, hh:hh + 1024]
                    xs_sem = "x0" if tt % 2 == 0 else "x1"
                    ctx.dma("sync", xs_sem, xsl, xb[l0 + tt * 128:l0 + (tt + 1) * 128, :],
                            waits=[("sv", xhat_vs[gtile - 2] if gtile >= 2 else 0)])
                    q = ctx.c(xs_sem)

                    def fbn(e, xsl=xsl):
                        e.bn_stats(out=stat[:, 0:6], in_=xsl[:, 0:512])
                        return e.bn_stats(out=stat[:, 6:12], in_=xsl[:, 512:1024])
                    bnv = ctx.do("vector", fbn, waits=[(xs_sem, q)])
                    agv = ctx.do("vector", lambda e: e.bn_aggr(
                        out=stat[:, 12:14], in_=stat[:, 0:12]),
                        waits=[("sv", bnv)])
                    ctx.do("vector", lambda e: e.tensor_scalar(
                        out=stat[:, 14:15], in0=stat[:, 13:14], scalar1=1.0 + EPS,
                        scalar2=EPS * EPS, op0=ALU.mult, op1=ALU.add),
                        waits=[("sv", agv)])
                    ctx.do("scalar", lambda e: e.activation(
                        out=stat[:, 15:16], in_=stat[:, 14:15], func=AF.Sqrt),
                        waits=[("sv", ctx.c("sv"))])
                    rc1 = ctx.do("vector", lambda e: e.reciprocal(
                        out=stat[:, 20:21], in_=stat[:, 15:16]),
                        waits=[("sa", ctx.c("sa"))])
                    vxh = ctx.do("vector", lambda e, xsl=xsl, xhl=xhl: e.tensor_scalar(
                        out=xhl, in0=xsl, scalar1=stat[:, 12:13], scalar2=stat[:, 20:21],
                        op0=ALU.subtract, op1=ALU.mult),
                        waits=[("st", trmm_sts[gtile - 2] if gtile >= 2 else 0),
                               ("sv", rc1)])
                    xhat_vs.append(vxh)
                    for kk in range(8):
                        gtr = gtile * 8 + kk
                        slot = (gtr % 2) * 1024
                        tm = ctx.do("tensor", lambda e, kk=kk, slot=slot, xhl=xhl: e.transpose(
                            ptr[:, slot:slot + 128], xhl[:, kk * 128:(kk + 1) * 128],
                            idt[:, :]),
                            waits=[("sv", vxh),
                                   ("wq", wq_done if gtr == 0 else 0),
                                   ("sa", trev_sas[gtr - 2] if gtr >= 2 else 0)])
                        ctx.do("scalar", lambda e, kk=kk, tt=tt, slot=slot: e.activation(
                            out=xnf[:, kk * 1024 + tt * 128:kk * 1024 + (tt + 1) * 128],
                            in_=ptr[:, slot:slot + 128], func=AF.Copy,
                            scale=nwc[:, kk:kk + 1]),
                            waits=[("st", tm)])
                        trev_sas.append(ctx.c("sa"))
                    trmm_sts.append(ctx.c("st"))
                xn_ev = ctx.c("sa")
                if TRUNC >= 1:
                    continue

                if ch > 0:
                    def halo(e):
                        for db in range(NDB):
                            inst = e.tensor_copy(
                                out=xi[:, db * (LC + 3):db * (LC + 3) + 3],
                                in_=xi[:, db * (LC + 3) + LC:db * (LC + 3) + LC + 3])
                        return inst
                    ctx.do("vector", halo, waits=[("st", pv["conv_mm"])])
                halo_v = ctx.c("sv")

                for ec in range(8):
                    for t2 in range(2):
                        def fmm(e, ec=ec, t2=t2):
                            for kk in range(8):
                                inst = e.matmul(
                                    pmm[:, :],
                                    w_in[:, kk * 1024 + ec * 128:kk * 1024 + (ec + 1) * 128],
                                    xnf[:, kk * 1024 + t2 * 512:kk * 1024 + (t2 + 1) * 512],
                                    start=(kk == 0), stop=(kk == 7))
                            return inst
                        mm = ctx.do("tensor", fmm,
                                    waits=[("sa", xn_ev if (ec == 0 and t2 == 0) else ctx.c("sa")),
                                           ("sv", halo_v if (ec == 0 and t2 == 0) else 0)])
                        if ec < 4:
                            ctx.do("scalar", lambda e, ec=ec, t2=t2: e.activation(
                                out=xi[:, ec * (LC + 3) + 3 + t2 * 512:
                                       ec * (LC + 3) + 3 + (t2 + 1) * 512],
                                in_=pmm[:, :], func=AF.Copy),
                                waits=[("st", mm)])
                        else:
                            ctx.do("scalar", lambda e, ec=ec, t2=t2: e.activation(
                                out=zb[:, (ec - 4) * LC + t2 * 512:
                                       (ec - 4) * LC + (t2 + 1) * 512],
                                in_=pmm[:, :], func=AF.Copy),
                                waits=[("st", mm)])
                pv["inproj_mm"] = ctx.c("st")
                inproj_ev = ctx.c("sa")

                for db in range(NDB):
                    for t2 in range(2):
                        def fcv(e, db=db, t2=t2):
                            for j in range(4):
                                inst = e.matmul(
                                    pmm[:, :],
                                    cdiag[:, (db * 4 + j) * 128:(db * 4 + j + 1) * 128],
                                    xi[:, db * (LC + 3) + j + t2 * 512:
                                       db * (LC + 3) + j + t2 * 512 + 512],
                                    start=(j == 0), stop=(j == 3))
                            return inst
                        mm = ctx.do("tensor", fcv,
                                    waits=[("sa", inproj_ev if (db == 0 and t2 == 0)
                                            else ctx.c("sa"))])
                        ctx.do("scalar", lambda e, db=db, t2=t2: e.activation(
                            out=xc[:, db * LC + t2 * 512:db * LC + (t2 + 1) * 512],
                            in_=pmm[:, :], func=AF.Silu, bias=cvb[:, db:db + 1]),
                            waits=[("st", mm),
                                   ("sv", pv["yg_v"] if (db == 0 and t2 == 0) else 0)])
                pv["conv_mm"] = ctx.c("st")
                conv_ev = ctx.c("sa")
                ctx.do("vector", lambda e: e.tensor_copy(out=xcb[:, :], in_=xc[:, :]),
                       waits=[("sa", conv_ev), ("st", pv["xcb_st"])])
                xcb_v = ctx.c("sv")

                for t2 in range(2):
                    def fxp(e, t2=t2):
                        for db in range(NDB):
                            inst = e.matmul(
                                pmm[0:96, :],
                                xp_w[:, db * 96:(db + 1) * 96],
                                xcb[:, db * LC + t2 * 512:db * LC + (t2 + 1) * 512],
                                start=(db == 0), stop=(db == 3))
                        return inst
                    mm = ctx.do("tensor", fxp,
                                waits=[("sv", xcb_v if t2 == 0 else 0),
                                       ("sa", ctx.c("sa"))])
                    ctx.do("scalar", lambda e, t2=t2: e.activation(
                        out=dbl_s[0:96, t2 * 512:(t2 + 1) * 512], in_=pmm[0:96, :],
                        func=AF.Copy),
                        waits=[("st", mm), ("ab", pv["ab_mark"] if t2 == 0 else 0)])
                pv["xcb_st"] = ctx.c("st")
                ctx.dma("gpsimd", "ab", arin[ch].ap(), dbl_s[0:96, :],
                        waits=[("sa", ctx.c("sa"))])
                pv["ab_mark"] = ctx.c("ab")
                ctx.do("gpsimd", lambda e, ch=ch: e.collective_compute(
                    "AllReduce", ALU.add,
                    replica_groups=[[0, 1, 2, 3], [4, 5, 6, 7]],
                    ins=[arin[ch].ap().opt()], outs=[arout[ch].ap().opt()]),
                    waits=[("ab", ctx.c("ab"))], sem="sc")
                arq = ctx.c("sc")
                ctx.dma("sync", "dr", dblr[0:64, :], arout[ch].ap()[0:64, :],
                        waits=[("sc", arq), ("sv", pv["dtin_v"])])
                ctx.dma("sync", "dr", bcrf[:, :], arout[ch].ap()[64:96, :])
                dblq = ctx.c("dr")

                ctx.do("vector", lambda e: e.tensor_copy(out=dtin[:, :], in_=dblr[0:64, :]),
                       waits=[("dr", dblq)])
                pv["dtin_v"] = ctx.c("sv")
                bcrb_v = ctx.do("vector", lambda e: e.tensor_copy(out=bcrb[:, :], in_=bcrf[:, :]))
                ctx.dma("gpsimd", "b1", bcr1[0:1, 0:11 * LC].rearrange("o (r c) -> o r c", r=11),
                        bcrb[0:11, :], waits=[("sv", bcrb_v), ("st", pv["bcr_mm"])])
                ctx.dma("gpsimd", "b1", bcr1[32:33, 0:11 * LC].rearrange("o (r c) -> o r c", r=11),
                        bcrb[11:22, :])
                ctx.dma("gpsimd", "b1", bcr1[64:65, 0:10 * LC].rearrange("o (r c) -> o r c", r=10),
                        bcrb[22:32, :])
                bcr1_q = ctx.c("b1")
                dtin_v = ctx.c("sv")
                for db in range(NDB):
                    for t2 in range(2):
                        mm = ctx.do("tensor", lambda e, db=db, t2=t2: e.matmul(
                            pmm[:, :], dt_w[:, db * 128:(db + 1) * 128],
                            dtin[:, t2 * 512:(t2 + 1) * 512], start=True, stop=True),
                            waits=[("sv", dtin_v if (db == 0 and t2 == 0) else 0),
                                   ("sa", ctx.c("sa"))])
                        spw = ctx.c("sv")
                        ctx.do("scalar", lambda e, db=db: e.activation(
                            out=sp1[:, :], in_=pmm[:, :], func=AF.Abs,
                            bias=dtb[:, db:db + 1]),
                            waits=[("st", mm), ("sv", spw)])
                        ctx.do("scalar", lambda e: e.activation(
                            out=sp2[:, :], in_=sp1[:, :], func=AF.Exp, scale=-1.0))
                        ctx.do("scalar", lambda e: e.activation(
                            out=sp1[:, :], in_=sp2[:, :], func=AF.Ln, bias=1.0))
                        spa = ctx.do("scalar", lambda e, db=db: e.activation(
                            out=sp2[:, :], in_=pmm[:, :], func=AF.Relu,
                            bias=dtb[:, db:db + 1]))
                        ctx.do("vector", lambda e, db=db, t2=t2: e.tensor_tensor(
                            out=dlt[:, db * LC + t2 * 512:db * LC + (t2 + 1) * 512],
                            in0=sp1[:, :], in1=sp2[:, :], op=ALU.add),
                            waits=[("sa", spa),
                                   ("sa", pv["scan_all_a"] if (db == 0 and t2 == 0) else 0)])
                dt_ev = ctx.c("sv")
                ctx.do("vector", lambda e: e.tensor_tensor(
                    out=dub[:, :], in0=dlt[:, :], in1=xc[:, :], op=ALU.mult))
                du_v = ctx.c("sv")

                for th in range(2):
                    t0 = th * 512
                    for n in range(NST):
                        gi_n = (ch * 2 + th) * NST + n
                        slot = (n % 2) * 512
                        bmm = ctx.do("tensor", lambda e, n=n, t0=t0: e.matmul(
                            pbc[:, :], ones32[(n // 11) * 32:(n // 11) * 32 + 1, :],
                            bcr1[(n // 11) * 32:(n // 11) * 32 + 1,
                                 (n % 11) * LC + t0:(n % 11) * LC + t0 + 512],
                            start=True, stop=True),
                            waits=[("sv", du_v if n == 0 and th == 0 else 0),
                                   ("b1", bcr1_q if n == 0 and th == 0 else 0),
                                   ("sa", ctx.c("sa"))])
                        bev = ctx.do("scalar", lambda e, slot=slot: e.activation(
                            out=bbc[:, slot:slot + 512], in_=pbc[:, :], func=AF.Copy),
                            waits=[("st", bmm),
                                   ("sv", build_vs[gi_n - 2] if gi_n >= 2 else 0)])
                        cmm = ctx.do("tensor", lambda e, n=n, t0=t0: e.matmul(
                            pbc[:, :], ones32[((16 + n) // 11) * 32:((16 + n) // 11) * 32 + 1, :],
                            bcr1[((16 + n) // 11) * 32:((16 + n) // 11) * 32 + 1,
                                 ((16 + n) % 11) * LC + t0:((16 + n) % 11) * LC + t0 + 512],
                            start=True, stop=True),
                            waits=[("sa", bev)])
                        cev = ctx.do("scalar", lambda e, slot=slot: e.activation(
                            out=cbc[:, slot:slot + 512], in_=pbc[:, :], func=AF.Copy),
                            waits=[("st", cmm),
                                   ("sv", prodn_vs[gi_n - 2] if gi_n >= 2 else 0)])
                        for db in range(NDB):
                            gi = ((ch * 2 + th) * NST + n) * NDB + db
                            dslot = (gi % 2) * 512
                            aev = ctx.do("scalar", lambda e, db=db, n=n, t0=t0, dslot=dslot:
                                         e.activation(
                                             out=dA[:, dslot:dslot + 512],
                                             in_=dlt[:, db * LC + t0:db * LC + t0 + 512],
                                             func=AF.Exp,
                                             scale=acol[:, db * NST + n:db * NST + n + 1]),
                                         waits=[("sv", max(dt_ev, scan_vs[gi - 2] if gi >= 2 else 0))])
                            ctx.do("vector", lambda e, db=db, t0=t0, slot=slot, dslot=dslot:
                                   e.tensor_tensor(
                                       out=dBu[:, dslot:dslot + 512],
                                       in0=dub[:, db * LC + t0:db * LC + t0 + 512],
                                       in1=bbc[:, slot:slot + 512], op=ALU.mult),
                                   waits=[("sa", max(aev, bev))])
                            sc_v = ctx.do("vector", lambda e, db=db, n=n, dslot=dslot:
                                          e.tensor_tensor_scan(
                                              out=hs[:, dslot:dslot + 512],
                                              data0=dA[:, dslot:dslot + 512],
                                              data1=dBu[:, dslot:dslot + 512],
                                              initial=st8[:, db * NST + n:db * NST + n + 1],
                                              op0=ALU.mult, op1=ALU.add),
                                          waits=[("sv", state_marks.get((db, n), 0))])
                            scan_vs.append(sc_v)
                            state_marks[(db, n)] = ctx.do(
                                "vector", lambda e, db=db, n=n, dslot=dslot: e.tensor_copy(
                                    out=st8[:, db * NST + n:db * NST + n + 1],
                                    in_=hs[:, dslot + 511:dslot + 512]),
                                waits=[("sv", sc_v)])
                            pr_v = ctx.do("vector", lambda e, dslot=dslot, slot=slot:
                                          e.tensor_tensor(
                                              out=prodb[:, dslot:dslot + 512],
                                              in0=hs[:, dslot:dslot + 512],
                                              in1=cbc[:, slot:slot + 512], op=ALU.mult),
                                          waits=[("sa", cev),
                                                 ("st", idmm_sts[gi - 2] if gi >= 2 else 0)])
                            ist = ctx.do("tensor", lambda e, db=db, n=n, dslot=dslot: e.matmul(
                                py[:, db * 512:db * 512 + 512], idt[:, :],
                                prodb[:, dslot:dslot + 512],
                                start=(n == 0), stop=(n == NST - 1)),
                                waits=[("sv", pr_v)])
                            idmm_sts.append(ist)
                            if db == NDB - 1:
                                build_vs.append(ctx.c("sv"))
                                prodn_vs.append(ctx.c("sv"))
                    scan_mm = ctx.c("st")
                    pv["scan_all_v"] = ctx.c("sv")
                    pv["scan_all_a"] = ctx.c("sa")
                    for db in range(NDB):
                        gg = (ch * 2 + th) * NDB + db
                        zslot = (db % 2) * 512
                        g1 = ctx.do("vector", lambda e, db=db, t0=t0: e.scalar_tensor_tensor(
                            out=y1[:, :], in0=xc[:, db * LC + t0:db * LC + t0 + 512],
                            scalar=dpc[:, db:db + 1], in1=py[:, db * 512:db * 512 + 512],
                            op0=ALU.mult, op1=ALU.add),
                            waits=[("st", scan_mm),
                                   ("yo", pv["yo_mark"] if (db == 0 and th == 0) else 0)])
                        za = ctx.do("scalar", lambda e, db=db, t0=t0, zslot=zslot: e.activation(
                            out=zact[:, zslot:zslot + 512],
                            in_=zb[:, db * LC + t0:db * LC + t0 + 512],
                            func=AF.Silu),
                            waits=[("sv", gate_vs[gg - 2] if gg >= 2 else 0)])
                        gv = ctx.do("vector", lambda e, db=db, t0=t0, zslot=zslot: e.tensor_tensor(
                            out=yg[:, db * LC + t0:db * LC + t0 + 512], in0=y1[:, :],
                            in1=zact[:, zslot:zslot + 512], op=ALU.mult),
                            waits=[("sa", za)])
                        gate_vs.append(gv)
                pv["yg_v"] = ctx.c("sv")
                pv["bcr_mm"] = ctx.c("st")

                for db in range(NDB):
                    ctx.dma("gpsimd", "yo", ys[ch, db * 128:(db + 1) * 128, :],
                            yg[:, db * LC:(db + 1) * LC],
                            waits=[("sv", pv["yg_v"])])
                pv["yo_mark"] = ctx.c("yo")

            if DBG:
                locs = dict(xnf=xnf, zb=zb, xc=xc, dblr=dblr, bcrf=bcrf, dlt=dlt,
                            xhat=xhat, stat=stat, dub=dub, st8=st8)
                for dname, (dap, sname) in dbg_outs.items():
                    ctx.dma("gpsimd", "yo", dap, locs[sname][:, :],
                            waits=[("sv", ctx.c("sv")), ("sa", ctx.c("sa")),
                                   ("st", ctx.c("st"))])
            ctx.wait("gpsimd", [("yo", ctx.c("yo"))])

        register(block, sems, program)
    return nc



def build_phase_b():
    nc = bass.Bass("TRN2", target_bir_lowering=False, debug=False,
                   detect_race_conditions=False)

    x_sl = nc.dram_tensor("x_sl", [2 * 1024, D], F32, kind="ExternalInput").ap()
    yrc = nc.dram_tensor("yrc", [2, DI, 1024], BF16, kind="ExternalInput").ap()
    wout_T = nc.dram_tensor("wout_T", [DI, D], F32, kind="ExternalInput").ap()
    qkv_T = nc.dram_tensor("qkv_T", [D, 1536], F32, kind="ExternalInput").ap()
    proj_T = nc.dram_tensor("proj_T", [D, D], F32, kind="ExternalInput").ap()
    rt_T = nc.dram_tensor("rt_T", [D, 8], F32, kind="ExternalInput").ap()
    w1h = nc.dram_tensor("w1h", [NE, D, FF], F32, kind="ExternalInput").ap()
    w2h = nc.dram_tensor("w2h", [NE, FF, D], F32, kind="ExternalInput").ap()
    l1w = nc.dram_tensor("l1w", [128, 8], F32, kind="ExternalInput").ap()
    l1b = nc.dram_tensor("l1b", [128, 8], F32, kind="ExternalInput").ap()
    l2w = nc.dram_tensor("l2w", [128, 8], F32, kind="ExternalInput").ap()
    l2b = nc.dram_tensor("l2b", [128, 8], F32, kind="ExternalInput").ap()
    ident = nc.dram_tensor("ident", [128, 128], BF16, kind="ExternalInput").ap()

    out = nc.dram_tensor("out", [2 * 1024, D], F32, kind="ExternalOutput").ap()
    x1sp = nc.dram_tensor("x1sp", [2 * 1024, D], F32)
    hasp = nc.dram_tensor("hasp", [2 * 1024, D], F32)

    from contextlib import ExitStack
    with ExitStack() as _stk:
        ent = _stk.enter_context
        AW1 = ent(nc.sbuf_tensor("AW1", [128, 16384], BF16))
        AW2 = ent(nc.sbuf_tensor("AW2", [128, 16384], BF16))
        AW3 = ent(nc.sbuf_tensor("AW3", [128, 16384], BF16))
        AF1 = ent(nc.sbuf_tensor("AF1", [128, 8192], F32))
        l2r = ent(nc.sbuf_tensor("l2r", [128, 16384], BF16))
        qkt = ent(nc.sbuf_tensor("qkt", [128, 2 * 1536], BF16))
        wrr = ent(nc.sbuf_tensor("wrr", [128, 64], BF16))
        wgt = ent(nc.sbuf_tensor("wgt", [128, 16 * 8], F32))
        stat = ent(nc.sbuf_tensor("stat", [128, 24], F32))
        sml = ent(nc.sbuf_tensor("sml", [128, 256], F32))
        idt = ent(nc.sbuf_tensor("idt", [128, 128], BF16))
        lw1 = ent(nc.sbuf_tensor("lw1", [128, 8], F32))
        lb1 = ent(nc.sbuf_tensor("lb1", [128, 8], F32))
        lw2 = ent(nc.sbuf_tensor("lw2", [128, 8], F32))
        lb2 = ent(nc.sbuf_tensor("lb2", [128, 8], F32))
        xtl = ent(nc.sbuf_tensor("xtl", [128, 1024], F32))
        otl = ent(nc.sbuf_tensor("otl", [128, 1024], F32))
        pA = ent(nc.psum_tensor("pA", [128, 2 * 512], F32))
        ptrB = ent(nc.psum_tensor("ptrB", [128, 2 * 1024], BF16))
        pr = ent(nc.psum_tensor("pr", [128, 512], F32))
        pB = ent(nc.psum_tensor("pB", [128, 2 * 512], F32))
        sems = {}
        for _sn in ("sv", "sa", "st", "wq", "x0", "x1", "y0", "y1", "sp", "rd", "oo"):
            sems[_sn] = ent(nc.semaphore(_sn))
        block = ent(nc.Block())

        wout_sb = AW1
        wq_sb = AW2
        l1f0, l1f1 = 12288, 13312
        at0, at1 = 14336, 15360
        wp_sb = AW3
        yld0, yld1 = 8192, 10240
        gh_c = 12288
        af_c = 13312
        l2h_c = 14336
        xs_c, ar_c, x1_c, ft0_c, ft1_c = 0, 1024, 2048, 3072, 4096

        def program(ctx: Ctx):
            for kk in range(16):
                ctx.dma("gpsimd", "wq", wout_sb[:, kk * 1024:(kk + 1) * 1024],
                        wout_T[kk * 128:(kk + 1) * 128, :])
            for kk in range(8):
                ctx.dma("gpsimd", "wq", wq_sb[:, kk * 1536:(kk + 1) * 1536],
                        qkv_T[kk * 128:(kk + 1) * 128, :])
            for kk in range(8):
                ctx.dma("gpsimd", "wq", wp_sb[:, kk * 1024:(kk + 1) * 1024],
                        proj_T[kk * 128:(kk + 1) * 128, :])
            for kk in range(8):
                ctx.dma("gpsimd", "wq", wrr[:, kk * 8:(kk + 1) * 8],
                        rt_T[kk * 128:(kk + 1) * 128, :])
            for dst, srct in [(idt[:, :], ident), (lw1[:, :], l1w), (lb1[:, :], l1b),
                              (lw2[:, :], l2w), (lb2[:, :], l2b)]:
                ctx.dma("sync", "x0", dst, srct)
            wqd = ctx.c("wq")
            xcd = ctx.c("x0")

            def ln_stats(src, dst_rstd):
                bn = ctx.do("vector", lambda e: (
                    e.bn_stats(out=stat[:, 0:6], in_=src[:, 0:512]),
                    e.bn_stats(out=stat[:, 6:12], in_=src[:, 512:1024]))[-1])
                ag = ctx.do("vector", lambda e: e.bn_aggr(out=stat[:, 12:14],
                                                          in_=stat[:, 0:12]),
                            waits=[("sv", bn)])
                ctx.do("vector", lambda e: e.tensor_scalar(
                    out=stat[:, 14:15], in0=stat[:, 13:14], scalar1=1.0, scalar2=EPS,
                    op0=ALU.mult, op1=ALU.add), waits=[("sv", ag)])
                ctx.do("scalar", lambda e: e.activation(
                    out=stat[:, 15:16], in_=stat[:, 14:15], func=AF.Sqrt),
                    waits=[("sv", ctx.c("sv"))])
                rc = ctx.do("vector", lambda e, dst_rstd=dst_rstd: e.reciprocal(
                    out=stat[:, dst_rstd:dst_rstd + 1], in_=stat[:, 15:16]),
                    waits=[("sa", ctx.c("sa"))])
                return rc

            trmarks = {}
            prev_yldw = {}
            b1_trmm = []

            for j in range(8):
                for b in range(2):
                    tt = b * 8 + j
                    ysem = "y0" if b == 0 else "y1"
                    ycol = yld0 if b == 0 else yld1
                    ctx.dma("sync", ysem,
                            AW3[:, ycol:ycol + 2048].rearrange("p (k c) -> p k c", k=16),
                            yrc[b].rearrange("(k p) L -> p k L", p=128)[:, :, j * 128:(j + 1) * 128],
                            waits=[("st", prev_yldw.get(b, 0))])
                    yq = ctx.c(ysem)
                    xsem = "x0" if tt % 2 == 0 else "x1"
                    ctx.dma("sync", xsem, AF1[:, xs_c:xs_c + 1024],
                            x_sl[b * 1024 + j * 128: b * 1024 + (j + 1) * 128, :],
                            waits=[("sv", trmarks.get("x_done", 0))])
                    xq = ctx.c(xsem)
                    ctx.wait("vector", [(xsem, xq)])
                    rc = ln_stats(AF1[:, xs_c:xs_c + 1024], 16)
                    ctx.do("vector", lambda e: e.tensor_scalar(
                        out=AF1[:, ar_c:ar_c + 1024], in0=AF1[:, xs_c:xs_c + 1024],
                        scalar1=stat[:, 12:13], scalar2=stat[:, 16:17],
                        op0=ALU.subtract, op1=ALU.mult),
                        waits=[("sv", rc)])
                    a_v = ctx.c("sv")
                    for eh in range(2):
                        def fmo(e, eh=eh, ycol=ycol):
                            for kk in range(16):
                                inst = e.matmul(
                                    pA[:, eh * 512:(eh + 1) * 512],
                                    AW3[:, ycol + kk * 128:ycol + (kk + 1) * 128],
                                    wout_sb[:, kk * 1024 + eh * 512:kk * 1024 + (eh + 1) * 512],
                                    start=(kk == 0), stop=(kk == 15))
                            return inst
                        ctx.do("tensor", fmo, waits=[(ysem, yq),
                                                     ("wq", wqd if tt == 0 else 0),
                                                     ("sa", ctx.c("sa")),
                                                     ("sv", trmarks.get("pA" + str(eh), 0))])
                    mo = ctx.c("st")
                    prev_yldw[b] = mo
                    def fret(e):
                        e.scalar_tensor_tensor(
                            out=AF1[:, ar_c:ar_c + 512], in0=pA[:, 0:512], scalar=1.0,
                            in1=AF1[:, ar_c:ar_c + 512], op0=ALU.mult, op1=ALU.add)
                        return e.scalar_tensor_tensor(
                            out=AF1[:, ar_c + 512:ar_c + 1024], in0=pA[:, 512:1024],
                            scalar=1.0, in1=AF1[:, ar_c + 512:ar_c + 1024],
                            op0=ALU.mult, op1=ALU.add)
                    ctx.do("vector", fret, waits=[("st", mo), ("sv", a_v)])
                    trmarks["pA0"] = trmarks["pA1"] = ctx.c("sv")
                    rc = ln_stats(AF1[:, ar_c:ar_c + 1024], 16)
                    ctx.do("vector", lambda e: e.tensor_scalar(
                        out=AF1[:, ar_c:ar_c + 1024], in0=AF1[:, ar_c:ar_c + 1024],
                        scalar1=stat[:, 12:13], scalar2=stat[:, 16:17],
                        op0=ALU.subtract, op1=ALU.mult), waits=[("sv", rc)])
                    x1v = ctx.do("vector", lambda e: e.tensor_tensor(
                        out=AF1[:, x1_c:x1_c + 1024], in0=AF1[:, xs_c:xs_c + 1024],
                        in1=AF1[:, ar_c:ar_c + 1024], op=ALU.add),
                        waits=[("sp", ctx.c("sp"))])
                    trmarks["x_done"] = x1v
                    ctx.dma("gpsimd", "sp", x1sp.ap()[b * 1024 + j * 128:b * 1024 + (j + 1) * 128, :],
                            AF1[:, x1_c:x1_c + 1024], waits=[("sv", x1v)])
                    ftc = ft0_c if b == 0 else ft1_c
                    rc = ln_stats(AF1[:, x1_c:x1_c + 1024], 16)
                    fv = ctx.do("vector", lambda e, ftc=ftc: e.tensor_scalar(
                        out=AF1[:, ftc:ftc + 1024], in0=AF1[:, x1_c:x1_c + 1024],
                        scalar1=stat[:, 12:13], scalar2=stat[:, 16:17],
                        op0=ALU.subtract, op1=ALU.mult), waits=[("sv", rc)])
                    rc = ln_stats(AF1[:, ftc:ftc + 1024], 16)
                    gv = ctx.do("vector", lambda e, ftc=ftc: e.tensor_scalar(
                        out=AW3[:, gh_c:gh_c + 1024], in0=AF1[:, ftc:ftc + 1024],
                        scalar1=stat[:, 12:13], scalar2=stat[:, 16:17],
                        op0=ALU.subtract, op1=ALU.mult),
                        waits=[("sv", rc), ("sa", trmarks.get("gh_a", 0))])
                    lcol = l1f0 if b == 0 else l1f1
                    for kk in range(8):
                        gi = len(b1_trmm)
                        slot = (gi % 2) * 1024
                        tm = ctx.do("tensor", lambda e, kk=kk, slot=slot: e.transpose(
                            ptrB[:, slot:slot + 128], AW3[:, gh_c + kk * 128:gh_c + (kk + 1) * 128],
                            idt[:, :]),
                            waits=[("sv", gv), ("x0", xcd if gi == 0 else 0),
                                   ("sa", b1_trmm[gi - 2] if gi >= 2 else 0)])
                        ctx.do("scalar", lambda e, kk=kk, slot=slot, lcol=lcol: e.activation(
                            out=AW2[:, lcol + kk * 128:lcol + (kk + 1) * 128],
                            in_=ptrB[:, slot:slot + 128], func=AF.Identity,
                            scale=lw1[:, kk:kk + 1], bias=lb1[:, kk:kk + 1]),
                            waits=[("st", tm)])
                        b1_trmm.append(ctx.c("sa"))
                    trmarks["gh_a"] = ctx.c("sa")
                    l1_done = ctx.c("sa")
                    for ec in range(3):
                        def fq(e, ec=ec, lcol=lcol):
                            for kk in range(8):
                                inst = e.matmul(
                                    pA[:, (ec % 2) * 512:(ec % 2) * 512 + 512],
                                    AW2[:, lcol + kk * 128:lcol + (kk + 1) * 128],
                                    wq_sb[:, kk * 1536 + ec * 512:kk * 1536 + (ec + 1) * 512],
                                    start=(kk == 0), stop=(kk == 7))
                            return inst
                        qm = ctx.do("tensor", fq, waits=[("sa", l1_done if ec == 0 else ctx.c("sa")),
                                                         ("sv", trmarks.get("pA" + str(ec % 2), 0))])
                        ctx.do("scalar", lambda e, ec=ec, b=b: e.activation(
                            out=qkt[:, b * 1536 + ec * 512:b * 1536 + (ec + 1) * 512],
                            in_=pA[:, (ec % 2) * 512:(ec % 2) * 512 + 512], func=AF.Copy),
                            waits=[("st", qm), ("sv", trmarks.get("qkt_v", 0) if ec == 0 else 0)])
                    trmarks["pA0"] = trmarks["pA1"] = 0
                    trmarks["qkv_a" + str(b)] = ctx.c("sa")

                ctx.wait("vector", [("sa", ctx.c("sa"))])
                q1o = 1536
                for h in range(NH):
                    kv = h // 4
                    for which, koff, ecol in ((0, 0, h), (1, q1o, 16 + h)):
                        pv1 = ctx.do("vector", lambda e, h=h, kv=kv, koff=koff: e.tensor_tensor(
                            out=sml[:, 32:96],
                            in0=qkt[:, q1o + h * 64:q1o + (h + 1) * 64],
                            in1=qkt[:, koff + 1024 + kv * 64:koff + 1024 + (kv + 1) * 64],
                            op=ALU.mult))
                        ctx.do("vector", lambda e, ecol=ecol: e.tensor_scalar(
                            out=sml[:, 96:160], in0=sml[:, 32:96], scalar1=0.125,
                            scalar2=0.0, op0=ALU.mult, op1=ALU.add,
                            accum_out=sml[:, ecol:ecol + 1]),
                            waits=[("sv", pv1)])
                ed = ctx.do("vector", lambda e: e.engine_nop() if False else e.tensor_copy(
                    out=sml[:, 224:225], in_=sml[:, 0:1]))
                mv = ctx.do("vector", lambda e: e.tensor_tensor(
                    out=sml[:, 160:176], in0=sml[:, 0:16], in1=sml[:, 16:32], op=ALU.max),
                    waits=[("sv", ed)])
                s1 = ctx.do("vector", lambda e: e.tensor_tensor(
                    out=sml[:, 176:192], in0=sml[:, 0:16], in1=sml[:, 160:176], op=ALU.subtract),
                    waits=[("sv", mv)])
                s2 = ctx.do("vector", lambda e: e.tensor_tensor(
                    out=sml[:, 192:208], in0=sml[:, 16:32], in1=sml[:, 160:176], op=ALU.subtract),
                    waits=[("sv", s1)])
                ex = ctx.do("scalar", lambda e: e.activation(
                    out=sml[:, 176:208], in_=sml[:, 176:208], func=AF.Exp),
                    waits=[("sv", s2)])
                sm = ctx.do("vector", lambda e: e.tensor_tensor(
                    out=sml[:, 208:224], in0=sml[:, 176:192], in1=sml[:, 192:208], op=ALU.add),
                    waits=[("sa", ex)])
                rcp = ctx.do("vector", lambda e: e.reciprocal(
                    out=sml[:, 160:176], in_=sml[:, 208:224]), waits=[("sv", sm)])
                p0 = ctx.do("vector", lambda e: e.tensor_tensor(
                    out=sml[:, 176:192], in0=sml[:, 176:192], in1=sml[:, 160:176], op=ALU.mult),
                    waits=[("sv", rcp)])
                p1 = ctx.do("vector", lambda e: e.tensor_tensor(
                    out=sml[:, 192:208], in0=sml[:, 192:208], in1=sml[:, 160:176], op=ALU.mult),
                    waits=[("sv", p0)])
                for h in range(NH):
                    kv = h // 4
                    ts1 = ctx.do("vector", lambda e, h=h, kv=kv: e.tensor_scalar(
                        out=sml[:, 32:96],
                        in0=qkt[:, 1280 + kv * 64:1280 + (kv + 1) * 64],
                        scalar1=sml[:, 176 + h:177 + h], scalar2=None, op0=ALU.mult),
                        waits=[("sv", p1 if h == 0 else 0)])
                    ctx.do("vector", lambda e, h=h, kv=kv: e.scalar_tensor_tensor(
                        out=AW2[:, at1 + h * 64:at1 + (h + 1) * 64],
                        in0=qkt[:, q1o + 1280 + kv * 64:q1o + 1280 + (kv + 1) * 64],
                        scalar=sml[:, 192 + h:193 + h], in1=sml[:, 32:96],
                        op0=ALU.mult, op1=ALU.add),
                        waits=[("sv", ts1)])
                    ctx.do("vector", lambda e, h=h, kv=kv: e.tensor_copy(
                        out=AW2[:, at0 + h * 64:at0 + (h + 1) * 64],
                        in_=qkt[:, 1280 + kv * 64:1280 + (kv + 1) * 64]))
                at_done = ctx.c("sv")
                trmarks["qkt_v"] = at_done

                for b in range(2):
                    tt = b * 8 + j
                    atc = at0 if b == 0 else at1
                    ftc = ft0_c if b == 0 else ft1_c
                    for kk in range(8):
                        gi = len(b1_trmm)
                        slot = (gi % 2) * 1024
                        tm = ctx.do("tensor", lambda e, kk=kk, slot=slot, atc=atc: e.transpose(
                            ptrB[:, slot:slot + 128], AW2[:, atc + kk * 128:atc + (kk + 1) * 128],
                            idt[:, :]),
                            waits=[("sv", at_done),
                                   ("sa", b1_trmm[gi - 2] if gi >= 2 else 0)])
                        ctx.do("scalar", lambda e, kk=kk, slot=slot: e.activation(
                            out=AW3[:, af_c + kk * 128:af_c + (kk + 1) * 128],
                            in_=ptrB[:, slot:slot + 128], func=AF.Copy),
                            waits=[("st", tm)])
                        b1_trmm.append(ctx.c("sa"))
                    af_done = ctx.c("sa")
                    for eh in range(2):
                        def fp(e, eh=eh):
                            for kk in range(8):
                                inst = e.matmul(
                                    pA[:, eh * 512:(eh + 1) * 512],
                                    AW3[:, af_c + kk * 128:af_c + (kk + 1) * 128],
                                    wp_sb[:, kk * 1024 + eh * 512:kk * 1024 + (eh + 1) * 512],
                                    start=(kk == 0), stop=(kk == 7))
                            return inst
                        pm = ctx.do("tensor", fp, waits=[("sa", af_done if eh == 0 else 0),
                                                         ("sv", trmarks.get("hA" + str(eh), 0))])
                        ctx.do("vector", lambda e, eh=eh, ftc=ftc: e.scalar_tensor_tensor(
                            out=AF1[:, ar_c + eh * 512:ar_c + (eh + 1) * 512],
                            in0=pA[:, eh * 512:(eh + 1) * 512], scalar=1.0,
                            in1=AF1[:, ftc + eh * 512:ftc + (eh + 1) * 512],
                            op0=ALU.mult, op1=ALU.add),
                            waits=[("st", pm), ("sp", ctx.c("sp") if eh == 0 else 0)])
                    havv = ctx.c("sv")
                    trmarks["hA0"] = trmarks["hA1"] = havv
                    ctx.dma("gpsimd", "sp", hasp.ap()[b * 1024 + j * 128:b * 1024 + (j + 1) * 128, :],
                            AF1[:, ar_c:ar_c + 1024], waits=[("sv", havv)])
                    rc = ln_stats(AF1[:, ar_c:ar_c + 1024], 16)
                    l2v = ctx.do("vector", lambda e: e.tensor_scalar(
                        out=AW3[:, l2h_c:l2h_c + 1024], in0=AF1[:, ar_c:ar_c + 1024],
                        scalar1=stat[:, 12:13], scalar2=stat[:, 16:17],
                        op0=ALU.subtract, op1=ALU.mult),
                        waits=[("sv", rc), ("sa", trmarks.get("l2_a", 0))])
                    for kk in range(8):
                        gi = len(b1_trmm)
                        slot = (gi % 2) * 1024
                        tm = ctx.do("tensor", lambda e, kk=kk, slot=slot: e.transpose(
                            ptrB[:, slot:slot + 128], AW3[:, l2h_c + kk * 128:l2h_c + (kk + 1) * 128],
                            idt[:, :]),
                            waits=[("sv", l2v),
                                   ("sa", b1_trmm[gi - 2] if gi >= 2 else 0)])
                        ctx.do("scalar", lambda e, kk=kk, slot=slot, tt=tt: e.activation(
                            out=l2r[:, kk * 2048 + tt * 128:kk * 2048 + (tt + 1) * 128],
                            in_=ptrB[:, slot:slot + 128], func=AF.Identity,
                            scale=lw2[:, kk:kk + 1], bias=lb2[:, kk:kk + 1]),
                            waits=[("st", tm)])
                        b1_trmm.append(ctx.c("sa"))
                    trmarks["l2_a"] = ctx.c("sa")
                    l2_done = ctx.c("sa")
                    def fr(e, tt=tt):
                        for kk in range(8):
                            inst = e.matmul(
                                pr[:, 0:8],
                                l2r[:, kk * 2048 + tt * 128:kk * 2048 + (tt + 1) * 128],
                                wrr[:, kk * 8:(kk + 1) * 8],
                                start=(kk == 0), stop=(kk == 7))
                        return inst
                    rm = ctx.do("tensor", fr, waits=[("sa", l2_done),
                                                     ("sv", trmarks.get("pr_v", 0))])
                    lg = ctx.do("vector", lambda e: e.tensor_copy(out=sml[:, 0:8],
                                                                  in_=pr[:, 0:8]),
                                waits=[("st", rm)])
                    trmarks["pr_v"] = lg
                    mx = ctx.do("vector", lambda e: e.tensor_reduce(
                        out=sml[:, 8:9], in_=sml[:, 0:8], axis=AX.X, op=ALU.max),
                        waits=[("sv", lg)])
                    sb = ctx.do("vector", lambda e: e.tensor_scalar(
                        out=sml[:, 0:8], in0=sml[:, 0:8], scalar1=sml[:, 8:9],
                        scalar2=None, op0=ALU.subtract), waits=[("sv", mx)])
                    exr = ctx.do("scalar", lambda e: e.activation(
                        out=sml[:, 0:8], in_=sml[:, 0:8], func=AF.Exp),
                        waits=[("sv", sb)])
                    sm2 = ctx.do("vector", lambda e: e.tensor_reduce(
                        out=sml[:, 8:9], in_=sml[:, 0:8], axis=AX.X, op=ALU.add),
                        waits=[("sa", exr)])
                    rcr = ctx.do("vector", lambda e: e.reciprocal(
                        out=sml[:, 9:10], in_=sml[:, 8:9]), waits=[("sv", sm2)])
                    gt = ctx.do("vector", lambda e: e.tensor_scalar(
                        out=sml[:, 0:8], in0=sml[:, 0:8], scalar1=sml[:, 9:10],
                        scalar2=None, op0=ALU.mult), waits=[("sv", rcr)])
                    m1 = ctx.do("vector", lambda e: e.tensor_reduce(
                        out=sml[:, 8:9], in_=sml[:, 0:8], axis=AX.X, op=ALU.max),
                        waits=[("sv", gt)])
                    eq = ctx.do("vector", lambda e: e.tensor_scalar(
                        out=sml[:, 16:24], in0=sml[:, 0:8], scalar1=sml[:, 8:9],
                        scalar2=None, op0=ALU.is_equal), waits=[("sv", m1)])
                    g2 = ctx.do("vector", lambda e: e.scalar_tensor_tensor(
                        out=sml[:, 16:24], in0=sml[:, 16:24], scalar=-1e30,
                        in1=sml[:, 0:8], op0=ALU.mult, op1=ALU.add),
                        waits=[("sv", eq)])
                    m2 = ctx.do("vector", lambda e: e.tensor_reduce(
                        out=sml[:, 8:9], in_=sml[:, 16:24], axis=AX.X, op=ALU.max),
                        waits=[("sv", g2)])
                    ge = ctx.do("vector", lambda e: e.tensor_scalar(
                        out=sml[:, 16:24], in0=sml[:, 0:8], scalar1=sml[:, 8:9],
                        scalar2=None, op0=ALU.is_ge), waits=[("sv", m2)])
                    ctx.do("vector", lambda e, tt=tt: e.tensor_tensor(
                        out=wgt[:, tt * 8:(tt + 1) * 8], in0=sml[:, 0:8],
                        in1=sml[:, 16:24], op=ALU.mult), waits=[("sv", ge)])

            b1v, b1a, b1t = ctx.c("sv"), ctx.c("sa"), ctx.c("st")
            b1sp = ctx.c("sp")
            for en in ("vector", "scalar", "tensor", "gpsimd", "sync"):
                ctx.wait(en, [("sv", b1v), ("sa", b1a), ("st", b1t), ("sp", b1sp)])

            w1_sb, hmid, w2_sb, acc = AW1, AW2, AW3, AF1
            w1ld = {}
            w2ld = {}
            for kk in range(8):
                ctx.dma("gpsimd", "wq", w1_sb[:, kk * 2048:(kk + 1) * 2048],
                        w1h[0, kk * 128:(kk + 1) * 128, :])
            w1ld[0] = ctx.c("wq")
            for fc in range(16):
                ctx.dma("gpsimd", "wq", w2_sb[:, fc * 1024:(fc + 1) * 1024],
                        w2h[0, fc * 128:(fc + 1) * 128, :])
            w2ld[0] = ctx.c("wq")

            for hf in range(2):
                for t4 in range(8):
                    tta = hf * 8 + t4
                    b2, j2 = tta // 8, tta % 8
                    ctx.dma("sync", "rd", acc[:, t4 * 1024:(t4 + 1) * 1024],
                            hasp.ap()[b2 * 1024 + j2 * 128:b2 * 1024 + (j2 + 1) * 128, :],
                            waits=[("oo", ctx.c("oo"))])
                ha_q = ctx.c("rd")
                ctx.wait("vector", [("rd", ha_q)])
                for e_ in range(NE):
                    w1mm_start = ctx.c("st")
                    for fc in range(16):
                        for t2 in range(2):
                            def fw1(e, fc=fc, t2=t2):
                                for kk in range(8):
                                    inst = e.matmul(
                                        pA[:, (fc % 2) * 512:(fc % 2) * 512 + 512],
                                        w1_sb[:, kk * 2048 + fc * 128:kk * 2048 + (fc + 1) * 128],
                                        l2r[:, kk * 2048 + hf * 1024 + t2 * 512:
                                            kk * 2048 + hf * 1024 + (t2 + 1) * 512],
                                        start=(kk == 0), stop=(kk == 7))
                                return inst
                            mm = ctx.do("tensor", fw1,
                                        waits=[("wq", w1ld[e_] if fc == 0 and t2 == 0 else 0),
                                               ("sa", ctx.c("sa"))])
                            ctx.do("scalar", lambda e, fc=fc, t2=t2: e.activation(
                                out=hmid[:, fc * 1024 + t2 * 512:fc * 1024 + (t2 + 1) * 512],
                                in_=pA[:, (fc % 2) * 512:(fc % 2) * 512 + 512],
                                func=AF.Gelu_apprx_tanh),
                                waits=[("st", mm)])
                    w1_done_mm = ctx.c("st")
                    hmid_a = ctx.c("sa")
                    ne, nhf = (e_ + 1, hf) if e_ < NE - 1 else (0, hf + 1)
                    if nhf < 2:
                        for kk in range(8):
                            ctx.dma("gpsimd", "wq", w1_sb[:, kk * 2048:(kk + 1) * 2048],
                                    w1h[ne, kk * 128:(kk + 1) * 128, :],
                                    waits=[("st", w1_done_mm)] if kk == 0 else [])
                        w1ld[(ne, nhf)] = w1ld[ne] = ctx.c("wq")
                    for t4 in range(8):
                        tta = hf * 8 + t4
                        for oh in range(2):
                            def fw2(e, t4=t4, oh=oh):
                                for fc in range(16):
                                    inst = e.matmul(
                                        pB[:, oh * 512:(oh + 1) * 512],
                                        hmid[:, fc * 1024 + t4 * 128:fc * 1024 + (t4 + 1) * 128],
                                        w2_sb[:, fc * 1024 + oh * 512:fc * 1024 + (oh + 1) * 512],
                                        start=(fc == 0), stop=(fc == 15))
                                return inst
                            mm = ctx.do("tensor", fw2,
                                        waits=[("wq", w2ld[e_] if t4 == 0 and oh == 0 else 0),
                                               ("sa", hmid_a if t4 == 0 and oh == 0 else 0),
                                               ("sv", trmarks.get("pB" + str(oh), 0))])
                            ctx.do("vector", lambda e, t4=t4, oh=oh, tta=tta, e_=e_: e.scalar_tensor_tensor(
                                out=acc[:, t4 * 1024 + oh * 512:t4 * 1024 + (oh + 1) * 512],
                                in0=pB[:, oh * 512:(oh + 1) * 512],
                                scalar=wgt[:, tta * 8 + e_:tta * 8 + e_ + 1],
                                in1=acc[:, t4 * 1024 + oh * 512:t4 * 1024 + (oh + 1) * 512],
                                op0=ALU.mult, op1=ALU.add),
                                waits=[("st", mm), ("rd", ha_q if e_ == 0 else 0)])
                            trmarks["pB" + str(oh)] = ctx.c("sv")
                    w2_done_mm = ctx.c("st")
                    if nhf < 2:
                        for fc in range(16):
                            ctx.dma("gpsimd", "wq", w2_sb[:, fc * 1024:(fc + 1) * 1024],
                                    w2h[ne, fc * 128:(fc + 1) * 128, :],
                                    waits=[("st", w2_done_mm)] if fc == 0 else [])
                        w2ld[ne] = ctx.c("wq")
                for t4 in range(8):
                    tta = hf * 8 + t4
                    b2, j2 = tta // 8, tta % 8
                    xr = "x0" if t4 % 2 == 0 else "x1"
                    ctx.dma("sync", xr, xtl[:, :],
                            x1sp.ap()[b2 * 1024 + j2 * 128:b2 * 1024 + (j2 + 1) * 128, :],
                            waits=[("sv", trmarks.get("xtl_v", 0))])
                    xq2 = ctx.c(xr)
                    rc = ln_stats(acc[:, t4 * 1024:(t4 + 1) * 1024], 16)
                    hv = ctx.do("vector", lambda e, t4=t4: e.tensor_scalar(
                        out=acc[:, t4 * 1024:(t4 + 1) * 1024],
                        in0=acc[:, t4 * 1024:(t4 + 1) * 1024],
                        scalar1=stat[:, 12:13], scalar2=stat[:, 16:17],
                        op0=ALU.subtract, op1=ALU.mult), waits=[("sv", rc)])
                    ov = ctx.do("vector", lambda e, t4=t4: e.tensor_tensor(
                        out=otl[:, :], in0=xtl[:, :], in1=acc[:, t4 * 1024:(t4 + 1) * 1024],
                        op=ALU.add),
                        waits=[(xr, xq2), ("oo", ctx.c("oo"))])
                    trmarks["xtl_v"] = ov
                    ctx.dma("gpsimd", "oo", out[b2 * 1024 + j2 * 128:b2 * 1024 + (j2 + 1) * 128, :],
                            otl[:, :], waits=[("sv", ov)])
            ctx.wait("gpsimd", [("oo", ctx.c("oo"))])

        register(block, sems, program)
    return nc


def prep_phase_b(inputs, ys_list):
    x = np.asarray(inputs["L0_attn_x"], np.float32)
    wout = np.asarray(inputs["L0_attn_out_proj_w"], np.float32)
    qkv = np.asarray(inputs["L0_ffn_qkv_weight"], np.float32)
    proj = np.asarray(inputs["L0_ffn_proj_weight"], np.float32)
    rt = np.asarray(inputs["L0_ffn_router_weight"], np.float32)
    w1 = np.asarray(inputs["L0_ffn_moe_w1"], np.float32)
    w2 = np.asarray(inputs["L0_ffn_moe_w2"], np.float32)
    def cols(v):
        return np.ascontiguousarray(np.asarray(v, np.float32).reshape(8, 128).T)
    ident = np.eye(128, dtype=np.float32).astype(BF)
    wout_T = np.ascontiguousarray(wout.T)
    qkv_T = np.ascontiguousarray(qkv.T)
    proj_T = np.ascontiguousarray(proj.T)
    rt_T = np.ascontiguousarray(rt.T)
    maps = []
    for c in range(8):
        yrc = np.empty((2, DI, 1024), BF)
        for b in range(2):
            for q in range(4):
                yrc[b, q * 512:(q + 1) * 512, :] = ys_list[b * 4 + q][c]
        xs = np.ascontiguousarray(
            x[:, c * 1024:(c + 1) * 1024, :].reshape(2 * 1024, D))
        maps.append({
            "x_sl": xs, "yrc": yrc, "wout_T": wout_T, "qkv_T": qkv_T,
            "proj_T": proj_T, "rt_T": rt_T, "w1h": w1, "w2h": w2,
            "l1w": cols(inputs["L0_ffn_ln1_weight"]),
            "l1b": cols(inputs["L0_ffn_ln1_bias"]),
            "l2w": cols(inputs["L0_ffn_ln2_weight"]),
            "l2b": cols(inputs["L0_ffn_ln2_bias"]),
            "ident": ident,
        })
    return maps


_NC_B = None


def run_phase_b(inputs, ys_list):
    global _NC_B
    if _NC_B is None:
        _NC_B = build_phase_b()
    maps = prep_phase_b(inputs, ys_list)
    res = run_bass_kernel_spmd(_NC_B, maps, core_ids=list(range(8)))
    return [r["out"] for r in res.results]


def prep_phase_a(inputs):
    x = np.ascontiguousarray(inputs["L0_attn_x"], np.float32)
    w_in = np.asarray(inputs["L0_attn_in_proj_w"], np.float32)
    conv_w = np.asarray(inputs["L0_attn_conv_w"], np.float32)
    conv_b = np.asarray(inputs["L0_attn_conv_b"], np.float32)
    xp_w = np.asarray(inputs["L0_attn_x_proj_w"], np.float32)
    dt_w = np.asarray(inputs["L0_attn_dt_proj_w"], np.float32)
    dt_b = np.asarray(inputs["L0_attn_dt_proj_b"], np.float32)
    a_log = np.asarray(inputs["L0_attn_A_log"], np.float32)
    dp = np.asarray(inputs["L0_attn_Dp"], np.float32)
    norm_w = np.asarray(inputs["L0_attn_norm_w"], np.float32)

    ident = np.eye(128, dtype=np.float32).astype(BF)
    w_in_T = np.ascontiguousarray(w_in.T)
    maps = []
    for c in range(8):
        b, q = c // 4, c % 4
        sl = slice(q * DQ, (q + 1) * DQ)
        cols = np.concatenate([w_in_T[:, q * DQ:(q + 1) * DQ],
                               w_in_T[:, DI + q * DQ:DI + (q + 1) * DQ]], axis=1)
        cd = np.zeros((128, NDB * 4 * 128), np.float32)
        for db in range(NDB):
            for j in range(4):
                blk = cd[:, (db * 4 + j) * 128:(db * 4 + j + 1) * 128]
                np.fill_diagonal(blk, conv_w[q * DQ + db * 128:q * DQ + (db + 1) * 128, j])
        def colmaj(v):
            return np.ascontiguousarray(v[sl].reshape(NDB, 128).T)
        maps.append({
            "xb": np.ascontiguousarray(x[b]),
            "w_in_T": np.ascontiguousarray(cols),
            "conv_diag": cd,
            "xp_T": np.ascontiguousarray(xp_w[:, sl].T),
            "dt_T": np.ascontiguousarray(dt_w[sl, :].T),
            "dt_b": colmaj(dt_b),
            "a_log": np.ascontiguousarray(
                a_log[sl].reshape(NDB, 128, NST).transpose(1, 0, 2).reshape(128, NDB * NST)),
            "dp_c": colmaj(dp),
            "convb_c": colmaj(conv_b),
            "normw_c": np.ascontiguousarray(norm_w.reshape(8, 128).T),
            "ident": ident,
        })
    return maps


_NC_A = None


def run_phase_a(inputs):
    global _NC_A
    if _NC_A is None:
        _NC_A = build_phase_a()
    maps = prep_phase_a(inputs)
    res = run_bass_kernel_spmd(_NC_A, maps, core_ids=list(range(8)))
    return [r["ys"] for r in res.results]


def kernel(**inputs):
    ys_list = run_phase_a(inputs)
    outs = run_phase_b(inputs, ys_list)
    full = np.empty((2, 8192, 1024), np.float32)
    for c in range(8):
        o = np.asarray(outs[c]).reshape(2, 1024, 1024)
        full[:, c * 1024:(c + 1) * 1024, :] = o
    return full
